# version 03
# baseline (speedup 1.0000x reference)
import numpy as np
import concourse.bass as bass
import concourse.mybir as mybir
from concourse.bass_utils import run_bass_kernel_spmd
from contextlib import ExitStack

F32 = mybir.dt.float32
BF16 = mybir.dt.bfloat16
I32 = mybir.dt.int32
AF = mybir.ActivationFunctionType
ALU = mybir.AluOpType
AX = mybir.AxisListType.X
LN2C = float(np.log(2.0))
LOG2E = float(1.0 / np.log(2.0))

T, D, FF = 2048, 1024, 4096
ND = 8
CH, NCH = 512, 4
CF, NCF = 256, 8
NHT = 32
OFF = 60.0


class Sem:
    def __init__(self, nc, stack, name):
        self.h = stack.enter_context(nc.semaphore(name))
        self.n = 0
        self.maxw = 0
        self.name = name

    def inc(self, instr, k=1):
        instr.then_inc(self.h, k)
        self.n += k
        return instr

    def incd(self, instr):
        return self.inc(instr, 16)

    def wait(self, eng, n=None):
        n = self.n if n is None else n
        if n > 0:
            self.maxw = max(self.maxw, n)
            eng.wait_ge(self.h, n)


def build_program():
    nc = bass.Bass("TRN2", target_bir_lowering=False)

    x_d = nc.dram_tensor("x", [T, D], F32, kind="ExternalInput").ap()
    w_d = {k: nc.dram_tensor(k, [D, D], F32, kind="ExternalInput").ap()
           for k in ["wq", "wk", "wv", "wu", "wo"]}
    b_d = {k: nc.dram_tensor(k, [D], F32, kind="ExternalInput").ap()
           for k in ["bq", "bk", "bv", "bu", "bo", "b2"]}
    lnw_d = nc.dram_tensor("lnw", [D], F32, kind="ExternalInput").ap()
    lnb_d = nc.dram_tensor("lnb", [D], F32, kind="ExternalInput").ap()
    w1_d = nc.dram_tensor("w1", [FF, D], F32, kind="ExternalInput").ap()
    b1_d = nc.dram_tensor("b1", [FF], F32, kind="ExternalInput").ap()
    w2_d = nc.dram_tensor("w2", [D, FF], F32, kind="ExternalInput").ap()
    out_d = nc.dram_tensor("out", [T, D], F32, kind="ExternalOutput").ap()

    s_x1 = nc.dram_tensor("s_x1", [T, D], BF16, kind="Internal").ap()
    s_uln = nc.dram_tensor("s_uln", [T, D], BF16, kind="Internal").ap()
    s_x2 = nc.dram_tensor("s_x2", [T, D], BF16, kind="Internal").ap()
    s_xmid = nc.dram_tensor("s_xmid", [T, D], F32, kind="Internal").ap()
    s_w1q = nc.dram_tensor("s_w1q", [FF, D], BF16, kind="Internal").ap()
    s_w2q = nc.dram_tensor("s_w2q", [D, FF], BF16, kind="Internal").ap()

    top = ExitStack()
    with top:
        sGD = Sem(nc, top, "sGD")
        sST = Sem(nc, top, "sST")
        sTL = Sem(nc, top, "sTL")
        sOUT = Sem(nc, top, "sOUT")
        sT = Sem(nc, top, "sT")
        sS = Sem(nc, top, "sS")
        sV = Sem(nc, top, "sV")
        sG = Sem(nc, top, "sG")
        sEs = Sem(nc, top, "sEs")
        sEv = Sem(nc, top, "sEv")
        sems = [sGD, sST, sTL, sOUT, sT, sS, sV, sG, sEs, sEv]

        def sb_(stack_, name, shape, dt):
            return stack_.enter_context(nc.sbuf_tensor(name, shape, dt))

        consts = sb_(top, "consts", [128, 8], F32)
        ones_bf = sb_(top, "ones_bf", [1, 512], BF16)
        ones_f = sb_(top, "ones_f", [1, 128], F32)
        b1row = sb_(top, "b1row", [1, FF], BF16)
        b2qrow = sb_(top, "b2qrow", [1, D], BF16)
        st_sum = sb_(top, "st_sum", [128, 4], F32)
        st_sqs = sb_(top, "st_sqs", [128, 4], F32)
        st_m = sb_(top, "st_m", [128, 4], F32)
        st_msq = sb_(top, "st_msq", [128, 4], F32)
        st_ex2 = sb_(top, "st_ex2", [128, 4], F32)
        st_var = sb_(top, "st_var", [128, 4], F32)
        st_sd = sb_(top, "st_sd", [128, 4], F32)
        st_r = sb_(top, "st_r", [128, 4], F32)
        st_mr = sb_(top, "st_mr", [128, 4], F32)
        sq_scr = sb_(top, "sq_scr", [128, 1024], F32)

        block = top.enter_context(nc.Block())

        class Chain:
            def __init__(self, stack_, tag, qw):
                self.leu = sb_(stack_, tag + "_leu", [128, qw], F32)
                self.i32 = sb_(stack_, tag + "_i32", [128, qw], I32)
                self.f1 = sb_(stack_, tag + "_f1", [128, qw], F32)
                self.s1 = sb_(stack_, tag + "_s1", [128, qw], F32)
                self.s2 = sb_(stack_, tag + "_s2", [128, qw], F32)
                self.vb = sb_(stack_, tag + "_vb", [128, qw], F32)
                self.f2 = sb_(stack_, tag + "_f2", [128, qw], F32)
                self.db = sb_(stack_, tag + "_db", [128, qw], F32)
                self.r1 = sb_(stack_, tag + "_r1", [128, qw], F32)
                self.qw = qw
                self.last_v1 = 0
                self.v2done = []

            def emit(self, a_ap, out_ap, man, a_waits=(), sg_ap=None,
                     add_ap=None, out_waits=(), part=128):
                minexp = -6 if man == 3 else -14
                cb1 = consts[:part, 1:2] if man == 3 else consts[:part, 5:6]
                cb2 = consts[:part, 3:4] if man == 3 else consts[:part, 6:7]
                qn = a_ap.shape[-1]
                leu = self.leu[:part, :qn]
                i32b = self.i32[:part, :qn]
                f1 = self.f1[:part, :qn]
                s1 = self.s1[:part, :qn]
                s2 = self.s2[:part, :qn]
                vb = self.vb[:part, :qn]
                f2 = self.f2[:part, :qn]
                db = self.db[:part, :qn]
                r1 = self.r1[:part, :qn]
                prev_v1 = self.last_v1

                @block.scalar
                def _(sc):
                    for sem, cnt in a_waits:
                        sem.wait(sc, cnt)
                    sV.wait(sc, prev_v1)
                    sS.inc(sc.activation(leu, a_ap, AF.Ln))

                sln = sS.n

                @block.vector
                def _(v):
                    sS.wait(v, sln)
                    v.tensor_scalar(vb, leu, LOG2E, OFF - 0.5, ALU.mult, ALU.add)
                    v.tensor_scalar(vb, vb, float(OFF + minexp), None, ALU.max)
                    v.tensor_copy(i32b, vb)
                    sV.inc(v.tensor_copy(f1, i32b))

                self.last_v1 = sV.n
                v1c = sV.n

                @block.scalar
                def _(sc):
                    sV.wait(sc, v1c)
                    sc.activation(s1, f1, AF.Exp, bias=cb1,
                                  scale=consts[:part, 2:3])
                    sS.inc(sc.activation(s2, f1, AF.Exp, bias=cb2,
                                         scale=consts[:part, 4:5]))

                sexp = sS.n

                @block.vector
                def _(v):
                    sS.wait(v, sexp)
                    for sem, cnt in out_waits:
                        sem.wait(v, cnt)
                    v.tensor_tensor(vb, a_ap, s1, ALU.mult)
                    v.tensor_copy(i32b, vb)
                    v.tensor_copy(f2, i32b)
                    v.tensor_tensor(db, vb, f2, ALU.subtract)
                    v.scalar_tensor_tensor(r1, db, 0.5, f2, ALU.is_ge, ALU.add)
                    v.scalar_tensor_tensor(f1, db, -0.5, r1, ALU.is_ge, ALU.add)
                    if sg_ap is None and add_ap is None:
                        last = v.scalar_tensor_tensor(out_ap, f1, -1.0, s2,
                                                      ALU.add, ALU.mult)
                    else:
                        v.scalar_tensor_tensor(db, f1, -1.0, s2, ALU.add, ALU.mult)
                        if sg_ap is not None and add_ap is not None:
                            v.tensor_tensor(r1, db, sg_ap, ALU.mult)
                            last = v.tensor_tensor(out_ap, r1, add_ap, ALU.add)
                        elif sg_ap is not None:
                            last = v.tensor_tensor(out_ap, db, sg_ap, ALU.mult)
                        else:
                            last = v.tensor_tensor(out_ap, db, add_ap, ALU.add)
                    sV.inc(last)

                self.v2done.append(sV.n)

        def pipelined_gemm(ngroups, psums, mm_emit, evac_engine, evac_emit,
                           evac_sem, pre_waits=()):
            tbase = sT.n
            es0, ev0 = sEs.n, sEv.n
            ebase = evac_sem.n

            @block.tensor
            def _(te):
                for sem, cnt in pre_waits:
                    sem.wait(te, cnt)
                sEs.wait(te, es0)
                sEv.wait(te, ev0)
                for k in range(ngroups):
                    if k >= 2:
                        evac_sem.wait(te, ebase + k - 1)
                    sT.inc(mm_emit(te, k, psums[k % 2]))

            def ebody(e):
                for k in range(ngroups):
                    sT.wait(e, tbase + k + 1)
                    evac_sem.inc(evac_emit(e, k, psums[k % 2]))

            if evac_engine == "scalar":
                block.scalar(ebody)
            else:
                block.vector(ebody)

        def emit_ln(x_tile, apply_emit, x_waits=(), apply_waits=()):
            prev_g = sG.n

            @block.vector
            def _(v):
                for sem, cnt in x_waits:
                    sem.wait(v, cnt)
                sG.wait(v, prev_g)
                for i in range(4):
                    last = v.reduce_sum(st_sum[:, i:i + 1], x_tile(i), axis=AX)
                sV.inc(last)

            vsum = sV.n

            @block.scalar
            def _(sc):
                for sem, cnt in x_waits:
                    sem.wait(sc, cnt)
                sG.wait(sc, prev_g)
                for i in range(4):
                    last = sc.activation(sq_scr[:], x_tile(i), AF.Square,
                                         accum_out=st_sqs[:, i:i + 1])
                sS.inc(last)

            ssqs = sS.n

            @block.gpsimd
            def _(g):
                sV.wait(g, vsum)
                sS.wait(g, ssqs)
                g.tensor_scalar(st_m[:], st_sum[:], 1.0 / 1024.0, None, ALU.mult)
                g.tensor_tensor(st_msq[:], st_m[:], st_m[:], ALU.mult)
                g.tensor_scalar(st_ex2[:], st_sqs[:], 1.0 / 1024.0, None, ALU.mult)
                sG.inc(g.tensor_tensor(st_var[:], st_ex2[:], st_msq[:],
                                       ALU.subtract))

            gvar = sG.n

            @block.scalar
            def _(sc):
                sG.wait(sc, gvar)
                sS.inc(sc.activation(st_sd[:], st_var[:], AF.Sqrt,
                                     bias=consts[:, 0:1]))

            ssd = sS.n

            @block.vector
            def _(v):
                sS.wait(v, ssd)
                sV.inc(v.reciprocal(st_r[:], st_sd[:]))

            vr = sV.n

            @block.gpsimd
            def _(g):
                sV.wait(g, vr)
                sG.inc(g.tensor_tensor(st_mr[:], st_m[:], st_r[:], ALU.mult))

            gmr = sG.n

            @block.vector
            def _(v):
                sG.wait(v, gmr)
                for sem, cnt in apply_waits:
                    sem.wait(v, cnt)
                for i in range(4):
                    apply_emit(v, i, st_r[:, i:i + 1], st_mr[:, i:i + 1])

        r1s = ExitStack()
        with r1s:
            brow = {k: sb_(r1s, "brow_" + k, [1, D], BF16)
                    for k in ["bq", "bk", "bv", "bu", "bo"]}
            b2f = sb_(r1s, "b2f", [1, D], F32)
            lnw_r = sb_(r1s, "lnw_r", [1, D], F32)
            lnb_r = sb_(r1s, "lnb_r", [1, D], F32)
            lnw_b = sb_(r1s, "lnw_b", [128, D], BF16)
            lnb_b = sb_(r1s, "lnb_b", [128, D], BF16)
            x_ch = sb_(r1s, "x_ch", [128, 4 * D], F32)
            x1T = sb_(r1s, "x1T", [128, ND * CH], BF16)
            wA = sb_(r1s, "wA", [128, ND * D], BF16)
            wB = sb_(r1s, "wB", [128, ND * D], BF16)
            k_sb = sb_(r1s, "k_sb", [128, 4 * D], BF16)
            v_sb = sb_(r1s, "v_sb", [128, 4 * D], BF16)
            qT_ch = sb_(r1s, "qT_ch", [128, ND * CH], BF16)
            aoW = sb_(r1s, "aoW", [128, 4 * D], F32)
            t1buf = sb_(r1s, "t1buf", [128, D], F32)
            t2buf = sb_(r1s, "t2buf", [128, D], F32)
            uln_ch = sb_(r1s, "uln_ch", [128, 4 * D], BF16)
            ulnT_ch = sb_(r1s, "ulnT_ch", [128, ND * CH], BF16)
            xbuf = sb_(r1s, "xbuf", [128, 4 * D], BF16)
            kvblk = sb_(r1s, "kvblk", [128, 8 * 128], BF16)
            kvsg = sb_(r1s, "kvsg", [128, 128], F32)
            kvt1 = sb_(r1s, "kvt1", [128, 128], F32)
            kvt2 = sb_(r1s, "kvt2", [128, 128], F32)
            w2in = sb_(r1s, "w2in", [128, 2 * 512], BF16)
            w2sg = sb_(r1s, "w2sg", [128, 512], F32)
            w2a = sb_(r1s, "w2a", [128, 512], F32)
            w2q_o = sb_(r1s, "w2q_o", [128, 2 * 512], BF16)
            ch_w2 = Chain(r1s, "cw2", 512)
            ps_kv = r1s.enter_context(nc.psum_tensor("ps_kv", [128, 1024], F32))
            ps_big = [r1s.enter_context(
                nc.psum_tensor(f"ps_big{i}", [128, 1024], F32)) for i in range(2)]
            ps_sm = [r1s.enter_context(
                nc.psum_tensor(f"ps_sm{i}", [128, 512], F32)) for i in range(2)]

            @block.vector
            def _(v):
                v.memset(consts[:, 0:1], 1e-5)
                v.memset(consts[:, 1:2], float((3 + OFF) * LN2C))
                v.memset(consts[:, 2:3], float(-LN2C))
                v.memset(consts[:, 3:4], float(-(3 + OFF) * LN2C))
                v.memset(consts[:, 4:5], float(LN2C))
                v.memset(consts[:, 5:6], float((10 + OFF) * LN2C))
                v.memset(consts[:, 6:7], float(-(10 + OFF) * LN2C))
                v.memset(ones_bf[:], 1.0)
                sV.inc(v.memset(ones_f[:], 1.0))

            consts_ready = sV.n

            @block.gpsimd
            def _(g):
                for dt in range(ND):
                    sGD.incd(g.dma_start(wA[:, dt * D:(dt + 1) * D],
                                         w_d["wk"][dt * 128:(dt + 1) * 128, :]))
                for dt in range(ND):
                    sGD.incd(g.dma_start(wB[:, dt * D:(dt + 1) * D],
                                         w_d["wv"][dt * 128:(dt + 1) * 128, :]))
                for k in ["bq", "bk", "bv", "bu", "bo"]:
                    sGD.incd(g.dma_start(brow[k][:], b_d[k][None, :]))
                sGD.incd(g.dma_start(b1row[:], b1_d[None, :]))
                sGD.incd(g.dma_start(b2f[:], b_d["b2"][None, :]))
                sGD.incd(g.dma_start(lnw_r[:], lnw_d[None, :]))
                sGD.incd(g.dma_start(lnb_r[:], lnb_d[None, :]))
                for i in range(4):
                    sGD.incd(g.dma_start(
                        s_w1q[i * FF // 4:(i + 1) * FF // 4, :],
                        w1_d[i * FF // 4:(i + 1) * FF // 4, :]))

            gd_wk = 16 * ND
            gd_wv = 16 * 2 * ND
            gd_rows = 16 * (2 * ND + 9)
            gd_w1 = sGD.n

            @block.tensor
            def _(te):
                sV.wait(te, consts_ready)
                sGD.wait(te, gd_rows)
                for h in range(2):
                    te.matmul(ps_big[0][:, h * 512:(h + 1) * 512],
                              ones_f[0:1, 0:128], lnw_r[0:1, h * 512:(h + 1) * 512],
                              start=True, stop=True)
                    te.matmul(ps_big[1][:, h * 512:(h + 1) * 512],
                              ones_f[0:1, 0:128], lnb_r[0:1, h * 512:(h + 1) * 512],
                              start=True, stop=True)
                te.matmul(ps_sm[0][:, 0:512], ones_f[0:1, 0:128],
                          b2f[0:1, 0:512], start=True, stop=True)
                sT.inc(te.matmul(ps_sm[1][:, 0:512], ones_f[0:1, 0:128],
                                 b2f[0:1, 512:1024], start=True, stop=True))

            pre_mm = sT.n

            @block.vector
            def _(v):
                sT.wait(v, pre_mm)
                v.tensor_copy(lnw_b[:], ps_big[0][:, 0:1024])
                v.tensor_copy(lnb_b[:], ps_big[1][:, 0:1024])
                v.tensor_copy(aoW[:, 0:512], ps_sm[0][:, 0:512])
                sEv.inc(v.tensor_copy(aoW[:, 512:1024], ps_sm[1][:, 0:512]))

            pre_bc = sEv.n

            @block.scalar
            def _(sc):
                sEv.wait(sc, pre_bc)
                sS.inc(sc.sign(aoW[:, 1024:2048], aoW[:, 0:1024]))

            b2sgc = sS.n

            @block.vector
            def _(v):
                sS.wait(v, b2sgc)
                sV.inc(v.tensor_tensor(aoW[:, 2048:3072], aoW[:, 0:1024],
                                       aoW[:, 1024:2048], ALU.mult))

            b2abc = sV.n
            for h in range(2):
                ch_w2.emit(aoW[:, 2048 + h * 512: 2048 + (h + 1) * 512],
                           w2q_o[:, h * 512:(h + 1) * 512], 10,
                           a_waits=[(sV, b2abc)],
                           sg_ap=aoW[:, 1024 + h * 512: 1024 + (h + 1) * 512])

            b2chain_v = sV.n

            @block.scalar
            def _(sc):
                sV.wait(sc, b2chain_v)
                sS.inc(sc.copy(b2qrow[0:1, :], w2q_o[0:1, 0:1024]))

            b2q_done_s = sS.n

            w2_tiles = [(pt, cc) for pt in range(8) for cc in range(8)]
            w2_state = {"idx": 0, "slot_free_v": [b2chain_v, b2chain_v],
                        "store_cnt": [0, 0], "sg_free_v": b2chain_v}

            def emit_w2_quant(ntiles):
                for _ in range(ntiles):
                    idx = w2_state["idx"]
                    if idx >= 64:
                        return
                    w2_state["idx"] += 1
                    pt, cc = w2_tiles[idx]
                    slot = idx % 2
                    sfree = w2_state["slot_free_v"][slot]
                    sgfree = w2_state["sg_free_v"]

                    @block.gpsimd
                    def _(g, pt=pt, cc=cc, slot=slot, sfree=sfree):
                        sV.wait(g, sfree)
                        sGD.incd(g.dma_start(
                            w2in[:, slot * 512:(slot + 1) * 512],
                            w2_d[pt * 128:(pt + 1) * 128,
                                 cc * 512:(cc + 1) * 512]))

                    ldc = sGD.n

                    @block.scalar
                    def _(sc, slot=slot, ldc=ldc, sgfree=sgfree):
                        sGD.wait(sc, ldc)
                        sV.wait(sc, sgfree)
                        sS.inc(sc.sign(w2sg[:],
                                       w2in[:, slot * 512:(slot + 1) * 512]))

                    sgc = sS.n

                    @block.vector
                    def _(v, slot=slot, sgc=sgc):
                        sS.wait(v, sgc)
                        sV.inc(v.tensor_tensor(
                            w2a[:], w2in[:, slot * 512:(slot + 1) * 512],
                            w2sg[:], ALU.mult))

                    ac = sV.n
                    w2_state["slot_free_v"][slot] = ac
                    ch_w2.emit(w2a[:], w2q_o[:, slot * 512:(slot + 1) * 512], 3,
                               a_waits=[(sV, ac)], sg_ap=w2sg[:],
                               out_waits=[(sST, w2_state["store_cnt"][slot]),
                                          (sS, b2q_done_s)])
                    w2_state["sg_free_v"] = sV.n
                    done = sV.n

                    @block.sync
                    def _(sy, pt=pt, cc=cc, slot=slot, done=done):
                        sV.wait(sy, done)
                        sST.incd(sy.dma_start(
                            s_w2q[pt * 128:(pt + 1) * 128,
                                  cc * 512:(cc + 1) * 512],
                            w2q_o[:, slot * 512:(slot + 1) * 512]))

                    w2_state["store_cnt"][slot] = sST.n

            x1_store = [0] * NCH
            for c in range(NCH):
                @block.gpsimd
                def _(g, c=c, vnow=sV.n, snow=sS.n):
                    sV.wait(g, vnow)
                    sS.wait(g, snow)
                    for i in range(4):
                        nt = 4 * c + i
                        sGD.incd(g.dma_start(
                            x_ch[:, i * D:(i + 1) * D],
                            x_d[nt * 128:(nt + 1) * 128, :]))

                gd_xc = sGD.n

                def ap_x(i):
                    return x_ch[:, i * D:(i + 1) * D]

                def ap_x1(i):
                    return xbuf[:, i * D:(i + 1) * D]

                def apply1(v, i, r_col, mr_col):
                    last = v.tensor_scalar(ap_x1(i), ap_x(i), r_col, mr_col,
                                           ALU.mult, ALU.subtract)
                    if i == 3:
                        sV.inc(last)

                emit_ln(ap_x, apply1, x_waits=[(sGD, gd_xc)],
                        apply_waits=[(sST, x1_store[c - 1] if c else 0)])
                x1c = sV.n

                @block.sync
                def _(sy, c=c, x1c=x1c):
                    sV.wait(sy, x1c)
                    for i in range(4):
                        nt = 4 * c + i
                        sST.incd(sy.dma_start(
                            s_x1[nt * 128:(nt + 1) * 128, :], ap_x1(i)))

                x1_store[c] = sST.n

                @block.sync
                def _(sy, c=c):
                    sST.wait(sy, x1_store[c])
                    sT.wait(sy, sT.n)
                    for dt in range(ND):
                        sTL.incd(sy.dma_start_transpose(
                            x1T[:, dt * CH:(dt + 1) * CH],
                            s_x1[c * CH:(c + 1) * CH, dt * 128:(dt + 1) * 128]))

                x1Tc = sTL.n

                def mk_kv(weight_sb, bias_name):
                    def mm(te, k, ps, weight_sb=weight_sb, bias_name=bias_name):
                        nsl, jc = divmod(k, 2)
                        for dt in range(ND):
                            te.matmul(
                                ps[:, 0:512],
                                x1T[:, dt * CH + nsl * 128:
                                    dt * CH + nsl * 128 + 128],
                                weight_sb[:, dt * D + jc * 512:
                                          dt * D + jc * 512 + 512],
                                start=(dt == 0), stop=False)
                        return te.matmul(
                            ps[:, 0:512], ones_bf[0:1, 0:128],
                            brow[bias_name][0:1, jc * 512:jc * 512 + 512],
                            start=False, stop=True)
                    return mm

                def ev_k(e, k, ps):
                    nsl, jc = divmod(k, 2)
                    return e.activation(
                        k_sb[:, nsl * D + jc * 512: nsl * D + jc * 512 + 512],
                        ps[:, 0:512], AF.Relu)

                pipelined_gemm(8, ps_sm, mk_kv(wA, "bk"), "scalar", ev_k, sEs,
                               pre_waits=[(sTL, x1Tc), (sGD, gd_wk),
                                          (sV, consts_ready)])

                def ev_v(e, k, ps):
                    nsl, jc = divmod(k, 2)
                    return e.tensor_copy(
                        v_sb[:, nsl * D + jc * 512: nsl * D + jc * 512 + 512],
                        ps[:, 0:512])

                pipelined_gemm(8, ps_sm, mk_kv(wB, "bv"), "vector", ev_v, sEv,
                               pre_waits=[(sGD, gd_wv)])

                kvev_s, kvev_v = sEs.n, sEv.n

                @block.tensor
                def _(te, c=c, kvev_s=kvev_s, kvev_v=kvev_v):
                    sEs.wait(te, kvev_s)
                    sEv.wait(te, kvev_v)
                    last = None
                    for p in range(8):
                        for nsl in range(4):
                            last = te.matmul(
                                ps_kv[:, p * 128:(p + 1) * 128],
                                k_sb[:, nsl * D + p * 128:
                                     nsl * D + p * 128 + 128],
                                v_sb[:, nsl * D + p * 128:
                                     nsl * D + p * 128 + 128],
                                start=(c == 0 and nsl == 0 and p % 4 == 0),
                                stop=(c == NCH - 1 and nsl == 3 and p % 4 == 3))
                    sT.inc(last)

                emit_w2_quant(10)

            kv_done = sT.n

            @block.vector
            def _(v):
                sV.inc(v.memset(kvblk[:], 0.0))

            for p in range(8):
                @block.scalar
                def _(sc, p=p, vnow=sV.n):
                    sT.wait(sc, kv_done)
                    sV.wait(sc, vnow)
                    sS.inc(sc.sign(kvsg[:], ps_kv[:, p * 128:(p + 1) * 128]))

                sgc = sS.n

                @block.vector
                def _(v, p=p, sgc=sgc):
                    sS.wait(v, sgc)
                    v.tensor_tensor(kvt1[:], ps_kv[:, p * 128:(p + 1) * 128],
                                    kvsg[:], ALU.mult)
                    v.tensor_scalar(kvt2[:], kvt1[:], 0.01, 100.0,
                                    ALU.max, ALU.min)
                    v.tensor_tensor(kvt1[:], kvt2[:], kvsg[:], ALU.mult)
                    v.tensor_copy(kvblk[0:64, p * 128:p * 128 + 64],
                                  kvt1[0:64, 0:64])
                    sV.inc(v.tensor_copy(
                        kvblk[64:128, p * 128 + 64:p * 128 + 128],
                        kvt1[64:128, 64:128]))

            kvblk_done = sV.n

            @block.gpsimd
            def _(g):
                sT.wait(g, kv_done)
                for dt in range(ND):
                    sGD.incd(g.dma_start(wA[:, dt * D:(dt + 1) * D],
                                         w_d["wq"][dt * 128:(dt + 1) * 128, :]))
                for dt in range(ND):
                    sGD.incd(g.dma_start(wB[:, dt * D:(dt + 1) * D],
                                         w_d["wu"][dt * 128:(dt + 1) * 128, :]))

            gd_wq = sGD.n - 16 * ND
            gd_wu = sGD.n

            uln_store = [0] * NCH
            for c in range(NCH):
                @block.sync
                def _(sy, c=c):
                    sT.wait(sy, sT.n)
                    for dt in range(ND):
                        sTL.incd(sy.dma_start_transpose(
                            x1T[:, dt * CH:(dt + 1) * CH],
                            s_x1[c * CH:(c + 1) * CH, dt * 128:(dt + 1) * 128]))

                x1Tc = sTL.n

                def mm_q(te, k, ps):
                    jt = k
                    for dt in range(ND):
                        te.matmul(
                            ps[:, 0:512],
                            wA[:, dt * D + jt * 128: dt * D + jt * 128 + 128],
                            x1T[:, dt * CH: dt * CH + 512],
                            start=(dt == 0), stop=False)
                    return te.matmul(
                        ps[:, 0:512], brow["bq"][0:1, jt * 128:jt * 128 + 128],
                        ones_bf[0:1, 0:512], start=False, stop=True)

                def ev_q(e, k, ps):
                    return e.activation(qT_ch[:, k * 512:(k + 1) * 512],
                                        ps[:, 0:512], AF.Relu)

                pipelined_gemm(8, ps_sm, mm_q, "scalar", ev_q, sEs,
                               pre_waits=[(sTL, x1Tc), (sGD, gd_wq),
                                          (sV, kvblk_done)])

                def mm_ao(te, k, ps):
                    nsl = k
                    last = None
                    for p in range(8):
                        last = te.matmul(
                            ps[:, p * 128:(p + 1) * 128],
                            qT_ch[:, p * 512 + nsl * 128:
                                  p * 512 + nsl * 128 + 128],
                            kvblk[:, p * 128:(p + 1) * 128],
                            start=(p % 4 == 0), stop=(p % 4 == 3))
                    return last

                def ev_ao(e, k, ps):
                    return e.tensor_copy(aoW[:, k * D:(k + 1) * D],
                                         ps[:, 0:1024])

                pipelined_gemm(4, ps_big, mm_ao, "vector", ev_ao, sEv)
                ao_done = sEv.n

                def ap_ao(i):
                    return aoW[:, i * D:(i + 1) * D]

                def apply_attn(v, i, r_col, mr_col):
                    v.tensor_scalar(t1buf[:], ap_ao(i), r_col, mr_col,
                                    ALU.mult, ALU.subtract)
                    v.tensor_tensor(t2buf[:], t1buf[:], lnw_b[:], ALU.mult)
                    last = v.tensor_tensor(ap_ao(i), t2buf[:], lnb_b[:],
                                           ALU.add)
                    if i == 3:
                        sV.inc(last)

                emit_ln(ap_ao, apply_attn, x_waits=[(sEv, ao_done)])
                t3_done = sV.n

                def mm_u(te, k, ps):
                    nsl, jc = divmod(k, 2)
                    for dt in range(ND):
                        te.matmul(
                            ps[:, 0:512],
                            x1T[:, dt * CH + nsl * 128:
                                dt * CH + nsl * 128 + 128],
                            wB[:, dt * D + jc * 512: dt * D + jc * 512 + 512],
                            start=(dt == 0), stop=False)
                    return te.matmul(
                        ps[:, 0:512], ones_bf[0:1, 0:128],
                        brow["bu"][0:1, jc * 512:jc * 512 + 512],
                        start=False, stop=True)

                def ev_u(e, k, ps):
                    nsl, jc = divmod(k, 2)
                    return e.tensor_tensor(
                        uln_ch[:, nsl * D + jc * 512: nsl * D + jc * 512 + 512],
                        ps[:, 0:512],
                        aoW[:, nsl * D + jc * 512: nsl * D + jc * 512 + 512],
                        ALU.mult)

                pipelined_gemm(8, ps_sm, mm_u, "vector", ev_u, sEv,
                               pre_waits=[(sGD, gd_wu), (sV, t3_done),
                                          (sST, uln_store[c - 1] if c else 0)])
                uln_done = sEv.n

                @block.sync
                def _(sy, c=c, uln_done=uln_done):
                    sEv.wait(sy, uln_done)
                    for i in range(4):
                        nt = 4 * c + i
                        sST.incd(sy.dma_start(
                            s_uln[nt * 128:(nt + 1) * 128, :],
                            uln_ch[:, i * D:(i + 1) * D]))

                uln_store[c] = sST.n
                emit_w2_quant(6)

            @block.gpsimd
            def _(g):
                sT.wait(g, sT.n)
                for dt in range(ND):
                    sGD.incd(g.dma_start(wA[:, dt * D:(dt + 1) * D],
                                         w_d["wo"][dt * 128:(dt + 1) * 128, :]))

            gd_wo = sGD.n

            xm_store = [0] * NCH
            x2_store = [0] * NCH
            ln2_apply = [0] * NCH
            ln2_sq = [0] * NCH
            for c in range(NCH):
                @block.gpsimd
                def _(g, c=c, vnow=sV.n, snow=sS.n, evnow=sEv.n):
                    sV.wait(g, vnow)
                    sS.wait(g, snow)
                    sEv.wait(g, evnow)
                    for i in range(4):
                        nt = 4 * c + i
                        sGD.incd(g.dma_start(
                            x_ch[:, i * D:(i + 1) * D],
                            x_d[nt * 128:(nt + 1) * 128, :]))

                gd_xc = sGD.n

                @block.sync
                def _(sy, c=c):
                    sST.wait(sy, uln_store[c])
                    sT.wait(sy, sT.n)
                    for dt in range(ND):
                        sTL.incd(sy.dma_start_transpose(
                            ulnT_ch[:, dt * CH:(dt + 1) * CH],
                            s_uln[c * CH:(c + 1) * CH,
                                  dt * 128:(dt + 1) * 128]))

                ulnTc = sTL.n

                def mm_z(te, k, ps):
                    nsl, jc = divmod(k, 2)
                    for dt in range(ND):
                        te.matmul(
                            ps[:, 0:512],
                            ulnT_ch[:, dt * CH + nsl * 128:
                                    dt * CH + nsl * 128 + 128],
                            wA[:, dt * D + jc * 512: dt * D + jc * 512 + 512],
                            start=(dt == 0), stop=False)
                    return te.matmul(
                        ps[:, 0:512], ones_bf[0:1, 0:128],
                        brow["bo"][0:1, jc * 512:jc * 512 + 512],
                        start=False, stop=True)

                def ev_z(e, k, ps):
                    nsl, jc = divmod(k, 2)
                    off = nsl * D + jc * 512
                    return e.tensor_tensor(aoW[:, off:off + 512], ps[:, 0:512],
                                           x_ch[:, off:off + 512], ALU.add)

                prevw = [(sGD, gd_xc)]
                if c > 0:
                    prevw += [(sST, max(xm_store[c - 1], x2_store[c - 1])),
                              (sV, ln2_apply[c - 1]), (sS, ln2_sq[c - 1])]
                pipelined_gemm(8, ps_sm, mm_z, "vector", ev_z, sEv,
                               pre_waits=[(sTL, ulnTc), (sGD, gd_wo)] + prevw)
                xmid_done = sEv.n

                @block.sync
                def _(sy, c=c, xmid_done=xmid_done):
                    sEv.wait(sy, xmid_done)
                    for i in range(4):
                        nt = 4 * c + i
                        sST.incd(sy.dma_start(
                            s_xmid[nt * 128:(nt + 1) * 128, :],
                            aoW[:, i * D:(i + 1) * D]))

                xm_store[c] = sST.n

                def ap_xm(i):
                    return aoW[:, i * D:(i + 1) * D]

                def ap_x2o(i):
                    return xbuf[:, i * D:(i + 1) * D]

                def apply2(v, i, r_col, mr_col):
                    last = v.tensor_scalar(ap_x2o(i), ap_xm(i), r_col, mr_col,
                                           ALU.mult, ALU.subtract)
                    if i == 3:
                        sV.inc(last)

                emit_ln(ap_xm, apply2,
                        x_waits=[(sEv, xmid_done)],
                        apply_waits=[(sST, x2_store[c - 1] if c
                                      else x1_store[NCH - 1])])
                ln2_apply[c] = sV.n
                ln2_sq[c] = sS.n
                x2c = sV.n

                @block.sync
                def _(sy, c=c, x2c=x2c):
                    sV.wait(sy, x2c)
                    for i in range(4):
                        nt = 4 * c + i
                        sST.incd(sy.dma_start(
                            s_x2[nt * 128:(nt + 1) * 128, :],
                            xbuf[:, i * D:(i + 1) * D]))

                x2_store[c] = sST.n

            emit_w2_quant(64)
            r1_final = dict(T=sT.n, S=sS.n, V=sV.n, G=sG.n, Es=sEs.n,
                            Ev=sEv.n, ST=sST.n, TL=sTL.n, GD=sGD.n)

        r2s = ExitStack()
        with r2s:
            w1T = sb_(r2s, "w1T", [128, ND * FF], BF16)
            w2qT = sb_(r2s, "w2qT", [128, NHT * D], BF16)
            x2T_ch = sb_(r2s, "x2T_ch", [128, ND * CF], BF16)
            hq = sb_(r2s, "hq", [128, NHT * CF], BF16)
            abuf = sb_(r2s, "abuf", [128, 2 * 512], BF16)
            ch_f = Chain(r2s, "chf", 512)
            osg = sb_(r2s, "osg", [128, 512], F32)
            oa = sb_(r2s, "oa", [128, 512], F32)
            xm_ch = sb_(r2s, "xm_ch", [128, 2 * D], F32)
            outb = sb_(r2s, "outb", [128, 2 * D], F32)
            ps_h = [r2s.enter_context(
                nc.psum_tensor(f"ps_h{i}", [128, 512], F32)) for i in range(2)]
            ps_o = [r2s.enter_context(
                nc.psum_tensor(f"ps_o{i}", [128, 1024], F32)) for i in range(2)]

            def fence(e):
                sT.wait(e, r1_final["T"])
                sS.wait(e, r1_final["S"])
                sV.wait(e, r1_final["V"])
                sG.wait(e, r1_final["G"])
                sEs.wait(e, r1_final["Es"])
                sEv.wait(e, r1_final["Ev"])

            @block.tensor
            def _(te):
                fence(te)

            @block.scalar
            def _(sc):
                fence(sc)

            @block.vector
            def _(v):
                fence(v)

            @block.gpsimd
            def _(g):
                fence(g)
                sST.wait(g, r1_final["ST"])

            @block.sync
            def _(sy):
                fence(sy)
                sST.wait(sy, r1_final["ST"])
                sGD.wait(sy, gd_w1)
                for dt in range(ND):
                    sTL.incd(sy.dma_start_transpose(
                        w1T[:, dt * FF:(dt + 1) * FF],
                        s_w1q[:, dt * 128:(dt + 1) * 128]))
                for ht in range(NHT):
                    sTL.incd(sy.dma_start_transpose(
                        w2qT[:, ht * D:(ht + 1) * D],
                        s_w2q[:, ht * 128:(ht + 1) * 128]))

            w12T_done = sTL.n

            out_store = 0
            for c in range(NCF):
                @block.sync
                def _(sy, c=c):
                    sT.wait(sy, sT.n)
                    for dt in range(ND):
                        sTL.incd(sy.dma_start_transpose(
                            x2T_ch[:, dt * CF:(dt + 1) * CF],
                            s_x2[c * CF:(c + 1) * CF,
                                 dt * 128:(dt + 1) * 128]))

                x2Tc = sTL.n

                @block.gpsimd
                def _(g, c=c, vnow=sV.n):
                    sV.wait(g, vnow)
                    for i in range(2):
                        nt = 2 * c + i
                        sGD.incd(g.dma_start(
                            xm_ch[:, i * D:(i + 1) * D],
                            s_xmid[nt * 128:(nt + 1) * 128, :]))

                xm_c = sGD.n

                tbase = sT.n
                ebase = sEs.n
                es0, ev0 = sEs.n, sEv.n

                @block.tensor
                def _(te, c=c, tbase=tbase, ebase=ebase, x2Tc=x2Tc,
                      es0=es0, ev0=ev0):
                    sTL.wait(te, x2Tc)
                    if c == 0:
                        sTL.wait(te, w12T_done)
                    sEs.wait(te, es0)
                    sEv.wait(te, ev0)
                    for k in range(NHT):
                        if k >= 2:
                            sEs.wait(te, ebase + k - 1)
                        for dt in range(ND):
                            te.matmul(
                                ps_h[k % 2][:, 0:CF],
                                w1T[:, dt * FF + k * 128:
                                    dt * FF + k * 128 + 128],
                                x2T_ch[:, dt * CF:(dt + 1) * CF],
                                start=(dt == 0), stop=False)
                        sT.inc(te.matmul(
                            ps_h[k % 2][:, 0:CF],
                            b1row[0:1, k * 128:k * 128 + 128],
                            ones_bf[0:1, 0:CF], start=False, stop=True))

                for b in range(16):
                    abuf_free = (ch_f.v2done[-2] if len(ch_f.v2done) >= 2
                                 else 0)

                    @block.scalar
                    def _(sc, b=b, tbase=tbase, abuf_free=abuf_free):
                        sV.wait(sc, abuf_free)
                        for j in range(2):
                            k = b * 2 + j
                            sT.wait(sc, tbase + k + 1)
                            sEs.inc(sc.activation(
                                abuf[:, (b % 2) * 512 + j * CF:
                                     (b % 2) * 512 + (j + 1) * CF],
                                ps_h[k % 2][:, 0:CF], AF.Relu))

                    ch_f.emit(abuf[:, (b % 2) * 512:(b % 2) * 512 + 512],
                              hq[:, b * 512:(b + 1) * 512], 3)

                hq_done = sV.n
                t2base = sT.n
                sgS = sS.n

                @block.tensor
                def _(te, c=c, hq_done=hq_done, sgS=sgS):
                    sV.wait(te, hq_done)
                    sS.wait(te, max(sgS, b2q_done_s))
                    for k in range(4):
                        ns, jc = divmod(k, 2)
                        ps = ps_o[ns]
                        for ht in range(NHT):
                            te.matmul(
                                ps[:, jc * 512:(jc + 1) * 512],
                                hq[:, ht * CF + ns * 128:
                                   ht * CF + ns * 128 + 128],
                                w2qT[:, ht * D + jc * 512:
                                     ht * D + jc * 512 + 512],
                                start=(ht == 0), stop=False)
                        sT.inc(te.matmul(
                            ps[:, jc * 512:(jc + 1) * 512],
                            ones_bf[0:1, 0:128],
                            b2qrow[0:1, jc * 512:jc * 512 + 512],
                            start=False, stop=True))

                for k in range(4):
                    ns, jc = divmod(k, 2)
                    psl = ps_o[ns][:, jc * 512:(jc + 1) * 512]
                    osg_free = ch_f.v2done[-1] if ch_f.v2done else 0

                    @block.scalar
                    def _(sc, k=k, psl=psl, t2base=t2base, osg_free=osg_free):
                        sT.wait(sc, t2base + k + 1)
                        sV.wait(sc, osg_free)
                        sS.inc(sc.sign(osg[:], psl))

                    sgc = sS.n

                    @block.vector
                    def _(v, k=k, psl=psl, sgc=sgc, xm_c=xm_c,
                          out_store=out_store):
                        sS.wait(v, sgc)
                        sGD.wait(v, xm_c)
                        sOUT.wait(v, out_store)
                        sV.inc(v.tensor_tensor(oa[:], psl, osg[:], ALU.mult))

                    oac = sV.n
                    ch_f.emit(oa[:],
                              outb[:, ns * D + jc * 512:
                                   ns * D + jc * 512 + 512],
                              10, a_waits=[(sV, oac)], sg_ap=osg[:],
                              add_ap=xm_ch[:, ns * D + jc * 512:
                                           ns * D + jc * 512 + 512])

                och_done = sV.n

                @block.sync
                def _(sy, c=c, och_done=och_done):
                    sV.wait(sy, och_done)
                    for i in range(2):
                        nt = 2 * c + i
                        sOUT.incd(sy.dma_start(
                            out_d[nt * 128:(nt + 1) * 128, :],
                            outb[:, i * D:(i + 1) * D]))

                out_store = sOUT.n

            @block.sync
            def _(sy):
                sOUT.wait(sy)

        for s in sems:
            assert s.maxw <= s.n, f"sem {s.name}: wait {s.maxw} > total {s.n}"

    return nc


_NC_CACHE = []


def kernel(**inputs):
    x = np.ascontiguousarray(np.asarray(inputs["L0_attn_query"],
                                        dtype=np.float32))
    name_map = {
        "wq": "L0_attn_Wq", "bq": "L0_attn_bq", "wk": "L0_attn_Wk",
        "bk": "L0_attn_bk", "wv": "L0_attn_Wv", "bv": "L0_attn_bv",
        "wu": "L0_attn_Wu", "bu": "L0_attn_bu", "wo": "L0_attn_Wo",
        "bo": "L0_attn_bo", "lnw": "L0_attn_ln_w", "lnb": "L0_attn_ln_b",
        "w1": "L0_ffn_w1", "b1": "L0_ffn_b1", "w2": "L0_ffn_w2",
        "b2": "L0_ffn_b2",
    }
    shared = {k: np.ascontiguousarray(np.asarray(inputs[vn], dtype=np.float32))
              for k, vn in name_map.items()}
    if not _NC_CACHE:
        _NC_CACHE.append(build_program())
    nc = _NC_CACHE[0]
    in_maps = []
    for i in range(8):
        m = {"x": np.ascontiguousarray(x[i])}
        m.update(shared)
        in_maps.append(m)
    res = run_bass_kernel_spmd(nc, in_maps, core_ids=list(range(8)))
    return np.stack([np.asarray(res.results[i]["out"]) for i in range(8)],
                    axis=0).astype(np.float32)


if __name__ == "__main__":
    print("building...")
    build_program()
    print("build ok")


# version 04
# speedup vs baseline: 1.0028x; 1.0028x over previous
import numpy as np
import concourse.bass as bass
import concourse.mybir as mybir
from concourse.bass_utils import run_bass_kernel_spmd
from contextlib import ExitStack

F32 = mybir.dt.float32
BF16 = mybir.dt.bfloat16
I32 = mybir.dt.int32
AF = mybir.ActivationFunctionType
ALU = mybir.AluOpType
AX = mybir.AxisListType.X
LN2C = float(np.log(2.0))
LOG2E = float(1.0 / np.log(2.0))

T, D, FF = 2048, 1024, 4096
ND = 8
CH, NCH = 512, 4
CF, NCF = 256, 8
NHT = 32
OFF = 60.0


class Sem:
    def __init__(self, nc, stack, name):
        self.h = stack.enter_context(nc.semaphore(name))
        self.n = 0
        self.maxw = 0
        self.name = name

    def inc(self, instr, k=1):
        instr.then_inc(self.h, k)
        self.n += k
        return instr

    def incd(self, instr):
        return self.inc(instr, 16)

    def wait(self, eng, n=None):
        n = self.n if n is None else n
        if n > 0:
            self.maxw = max(self.maxw, n)
            eng.wait_ge(self.h, n)


def build_program():
    nc = bass.Bass("TRN2", target_bir_lowering=False)

    x_d = nc.dram_tensor("x", [T, D], F32, kind="ExternalInput").ap()
    w_d = {k: nc.dram_tensor(k, [D, D], F32, kind="ExternalInput").ap()
           for k in ["wq", "wk", "wv", "wu", "wo"]}
    b_d = {k: nc.dram_tensor(k, [D], F32, kind="ExternalInput").ap()
           for k in ["bq", "bk", "bv", "bu", "bo", "b2"]}
    lnw_d = nc.dram_tensor("lnw", [D], F32, kind="ExternalInput").ap()
    lnb_d = nc.dram_tensor("lnb", [D], F32, kind="ExternalInput").ap()
    w1_d = nc.dram_tensor("w1", [FF, D], F32, kind="ExternalInput").ap()
    b1_d = nc.dram_tensor("b1", [FF], F32, kind="ExternalInput").ap()
    w2_d = nc.dram_tensor("w2", [D, FF], F32, kind="ExternalInput").ap()
    out_d = nc.dram_tensor("out", [T, D], F32, kind="ExternalOutput").ap()

    s_x1 = nc.dram_tensor("s_x1", [T, D], BF16, kind="Internal").ap()
    s_uln = nc.dram_tensor("s_uln", [T, D], BF16, kind="Internal").ap()
    s_x2 = nc.dram_tensor("s_x2", [T, D], BF16, kind="Internal").ap()
    s_xmid = nc.dram_tensor("s_xmid", [T, D], F32, kind="Internal").ap()
    s_w1q = nc.dram_tensor("s_w1q", [FF, D], BF16, kind="Internal").ap()
    s_w2q = nc.dram_tensor("s_w2q", [D, FF], BF16, kind="Internal").ap()

    top = ExitStack()
    with top:
        sGD = Sem(nc, top, "sGD")
        sST = Sem(nc, top, "sST")
        sTL = Sem(nc, top, "sTL")
        sOUT = Sem(nc, top, "sOUT")
        sT = Sem(nc, top, "sT")
        sS = Sem(nc, top, "sS")
        sV = Sem(nc, top, "sV")
        sG = Sem(nc, top, "sG")
        sEs = Sem(nc, top, "sEs")
        sEv = Sem(nc, top, "sEv")
        sWA = Sem(nc, top, "sWA")
        sWB = Sem(nc, top, "sWB")
        sROW = Sem(nc, top, "sROW")
        sW1Q = Sem(nc, top, "sW1Q")
        sX = Sem(nc, top, "sX")
        sW2L = [Sem(nc, top, "sW2La"), Sem(nc, top, "sW2Lb")]
        sW2S = [Sem(nc, top, "sW2Sa"), Sem(nc, top, "sW2Sb")]
        sems = [sGD, sST, sTL, sOUT, sT, sS, sV, sG, sEs, sEv,
                sWA, sWB, sROW, sW1Q, sX] + sW2L + sW2S

        def sb_(stack_, name, shape, dt):
            return stack_.enter_context(nc.sbuf_tensor(name, shape, dt))

        consts = sb_(top, "consts", [128, 8], F32)
        ones_bf = sb_(top, "ones_bf", [1, 512], BF16)
        ones_f = sb_(top, "ones_f", [1, 128], F32)
        b1row = sb_(top, "b1row", [1, FF], BF16)
        b2qrow = sb_(top, "b2qrow", [1, D], BF16)
        st_sum = sb_(top, "st_sum", [128, 4], F32)
        st_sqs = sb_(top, "st_sqs", [128, 4], F32)
        st_m = sb_(top, "st_m", [128, 4], F32)
        st_msq = sb_(top, "st_msq", [128, 4], F32)
        st_ex2 = sb_(top, "st_ex2", [128, 4], F32)
        st_var = sb_(top, "st_var", [128, 4], F32)
        st_sd = sb_(top, "st_sd", [128, 4], F32)
        st_r = sb_(top, "st_r", [128, 4], F32)
        st_mr = sb_(top, "st_mr", [128, 4], F32)
        sq_scr = sb_(top, "sq_scr", [128, 1024], F32)

        block = top.enter_context(nc.Block())

        class Chain:
            def __init__(self, stack_, tag, qw):
                self.leu = sb_(stack_, tag + "_leu", [128, qw], F32)
                self.i32 = sb_(stack_, tag + "_i32", [128, qw], I32)
                self.f1 = sb_(stack_, tag + "_f1", [128, qw], F32)
                self.s1 = sb_(stack_, tag + "_s1", [128, qw], F32)
                self.s2 = sb_(stack_, tag + "_s2", [128, qw], F32)
                self.vb = sb_(stack_, tag + "_vb", [128, qw], F32)
                self.f2 = sb_(stack_, tag + "_f2", [128, qw], F32)
                self.db = sb_(stack_, tag + "_db", [128, qw], F32)
                self.r1 = sb_(stack_, tag + "_r1", [128, qw], F32)
                self.qw = qw
                self.last_v1 = 0
                self.v2done = []

            def emit(self, a_ap, out_ap, man, a_waits=(), sg_ap=None,
                     add_ap=None, out_waits=(), part=128):
                minexp = -6 if man == 3 else -14
                cb1 = consts[:part, 1:2] if man == 3 else consts[:part, 5:6]
                cb2 = consts[:part, 3:4] if man == 3 else consts[:part, 6:7]
                qn = a_ap.shape[-1]
                leu = self.leu[:part, :qn]
                i32b = self.i32[:part, :qn]
                f1 = self.f1[:part, :qn]
                s1 = self.s1[:part, :qn]
                s2 = self.s2[:part, :qn]
                vb = self.vb[:part, :qn]
                f2 = self.f2[:part, :qn]
                db = self.db[:part, :qn]
                r1 = self.r1[:part, :qn]
                prev_v1 = self.last_v1

                @block.scalar
                def _(sc):
                    for sem, cnt in a_waits:
                        sem.wait(sc, cnt)
                    sV.wait(sc, prev_v1)
                    sS.inc(sc.activation(leu, a_ap, AF.Ln))

                sln = sS.n

                @block.vector
                def _(v):
                    sS.wait(v, sln)
                    v.tensor_scalar(vb, leu, LOG2E, OFF - 0.5, ALU.mult, ALU.add)
                    v.tensor_scalar(vb, vb, float(OFF + minexp), None, ALU.max)
                    v.tensor_copy(i32b, vb)
                    sV.inc(v.tensor_copy(f1, i32b))

                self.last_v1 = sV.n
                v1c = sV.n

                @block.scalar
                def _(sc):
                    sV.wait(sc, v1c)
                    sc.activation(s1, f1, AF.Exp, bias=cb1,
                                  scale=consts[:part, 2:3])
                    sS.inc(sc.activation(s2, f1, AF.Exp, bias=cb2,
                                         scale=consts[:part, 4:5]))

                sexp = sS.n

                @block.vector
                def _(v):
                    sS.wait(v, sexp)
                    for sem, cnt in out_waits:
                        sem.wait(v, cnt)
                    v.tensor_tensor(vb, a_ap, s1, ALU.mult)
                    v.tensor_copy(i32b, vb)
                    v.tensor_copy(f2, i32b)
                    v.tensor_tensor(db, vb, f2, ALU.subtract)
                    v.scalar_tensor_tensor(r1, db, 0.5, f2, ALU.is_ge, ALU.add)
                    v.scalar_tensor_tensor(f1, db, -0.5, r1, ALU.is_ge, ALU.add)
                    if sg_ap is None and add_ap is None:
                        last = v.scalar_tensor_tensor(out_ap, f1, -1.0, s2,
                                                      ALU.add, ALU.mult)
                    else:
                        v.scalar_tensor_tensor(db, f1, -1.0, s2, ALU.add, ALU.mult)
                        if sg_ap is not None and add_ap is not None:
                            v.tensor_tensor(r1, db, sg_ap, ALU.mult)
                            last = v.tensor_tensor(out_ap, r1, add_ap, ALU.add)
                        elif sg_ap is not None:
                            last = v.tensor_tensor(out_ap, db, sg_ap, ALU.mult)
                        else:
                            last = v.tensor_tensor(out_ap, db, add_ap, ALU.add)
                    sV.inc(last)

                self.v2done.append(sV.n)

        def pipelined_gemm(ngroups, psums, mm_emit, evac_engine, evac_emit,
                           evac_sem, pre_waits=()):
            tbase = sT.n
            es0, ev0 = sEs.n, sEv.n
            ebase = evac_sem.n

            @block.tensor
            def _(te):
                for sem, cnt in pre_waits:
                    sem.wait(te, cnt)
                sEs.wait(te, es0)
                sEv.wait(te, ev0)
                for k in range(ngroups):
                    if k >= 2:
                        evac_sem.wait(te, ebase + k - 1)
                    sT.inc(mm_emit(te, k, psums[k % 2]))

            def ebody(e):
                for k in range(ngroups):
                    sT.wait(e, tbase + k + 1)
                    evac_sem.inc(evac_emit(e, k, psums[k % 2]))

            if evac_engine == "scalar":
                block.scalar(ebody)
            else:
                block.vector(ebody)

        def emit_ln(x_tile, apply_emit, x_waits=(), apply_waits=()):
            prev_g = sG.n

            @block.vector
            def _(v):
                for sem, cnt in x_waits:
                    sem.wait(v, cnt)
                sG.wait(v, prev_g)
                for i in range(4):
                    last = v.reduce_sum(st_sum[:, i:i + 1], x_tile(i), axis=AX)
                sV.inc(last)

            vsum = sV.n

            @block.scalar
            def _(sc):
                for sem, cnt in x_waits:
                    sem.wait(sc, cnt)
                sG.wait(sc, prev_g)
                for i in range(4):
                    last = sc.activation(sq_scr[:], x_tile(i), AF.Square,
                                         accum_out=st_sqs[:, i:i + 1])
                sS.inc(last)

            ssqs = sS.n

            @block.gpsimd
            def _(g):
                sV.wait(g, vsum)
                sS.wait(g, ssqs)
                g.tensor_scalar(st_m[:], st_sum[:], 1.0 / 1024.0, None, ALU.mult)
                g.tensor_tensor(st_msq[:], st_m[:], st_m[:], ALU.mult)
                g.tensor_scalar(st_ex2[:], st_sqs[:], 1.0 / 1024.0, None, ALU.mult)
                sG.inc(g.tensor_tensor(st_var[:], st_ex2[:], st_msq[:],
                                       ALU.subtract))

            gvar = sG.n

            @block.scalar
            def _(sc):
                sG.wait(sc, gvar)
                sS.inc(sc.activation(st_sd[:], st_var[:], AF.Sqrt,
                                     bias=consts[:, 0:1]))

            ssd = sS.n

            @block.vector
            def _(v):
                sS.wait(v, ssd)
                sV.inc(v.reciprocal(st_r[:], st_sd[:]))

            vr = sV.n

            @block.gpsimd
            def _(g):
                sV.wait(g, vr)
                sG.inc(g.tensor_tensor(st_mr[:], st_m[:], st_r[:], ALU.mult))

            gmr = sG.n

            @block.vector
            def _(v):
                sG.wait(v, gmr)
                for sem, cnt in apply_waits:
                    sem.wait(v, cnt)
                for i in range(4):
                    apply_emit(v, i, st_r[:, i:i + 1], st_mr[:, i:i + 1])

        r1s = ExitStack()
        with r1s:
            brow = {k: sb_(r1s, "brow_" + k, [1, D], BF16)
                    for k in ["bq", "bk", "bv", "bu", "bo"]}
            b2f = sb_(r1s, "b2f", [1, D], F32)
            lnw_r = sb_(r1s, "lnw_r", [1, D], F32)
            lnb_r = sb_(r1s, "lnb_r", [1, D], F32)
            lnw_b = sb_(r1s, "lnw_b", [128, D], BF16)
            lnb_b = sb_(r1s, "lnb_b", [128, D], BF16)
            x_ch = sb_(r1s, "x_ch", [128, 4 * D], F32)
            x1T = sb_(r1s, "x1T", [128, ND * CH], BF16)
            wA = sb_(r1s, "wA", [128, ND * D], BF16)
            wB = sb_(r1s, "wB", [128, ND * D], BF16)
            k_sb = sb_(r1s, "k_sb", [128, 4 * D], BF16)
            v_sb = sb_(r1s, "v_sb", [128, 4 * D], BF16)
            qT_ch = sb_(r1s, "qT_ch", [128, ND * CH], BF16)
            aoW = sb_(r1s, "aoW", [128, 4 * D], F32)
            t1buf = sb_(r1s, "t1buf", [128, D], F32)
            t2buf = sb_(r1s, "t2buf", [128, D], F32)
            uln_ch = sb_(r1s, "uln_ch", [128, 4 * D], BF16)
            ulnT_ch = sb_(r1s, "ulnT_ch", [128, ND * CH], BF16)
            xbuf = sb_(r1s, "xbuf", [128, 4 * D], BF16)
            kvblk = sb_(r1s, "kvblk", [128, 8 * 128], BF16)
            kvsg = sb_(r1s, "kvsg", [128, 128], F32)
            kvt1 = sb_(r1s, "kvt1", [128, 128], F32)
            kvt2 = sb_(r1s, "kvt2", [128, 128], F32)
            w2in = sb_(r1s, "w2in", [128, 2 * 512], BF16)
            w2sg = sb_(r1s, "w2sg", [128, 512], F32)
            w2a = sb_(r1s, "w2a", [128, 512], F32)
            w2q_o = sb_(r1s, "w2q_o", [128, 2 * 512], BF16)
            ch_w2 = Chain(r1s, "cw2", 512)
            ps_kv = r1s.enter_context(nc.psum_tensor("ps_kv", [128, 1024], F32))
            ps_big = [r1s.enter_context(
                nc.psum_tensor(f"ps_big{i}", [128, 1024], F32)) for i in range(2)]
            ps_sm = [r1s.enter_context(
                nc.psum_tensor(f"ps_sm{i}", [128, 512], F32)) for i in range(2)]

            @block.vector
            def _(v):
                v.memset(consts[:, 0:1], 1e-5)
                v.memset(consts[:, 1:2], float((3 + OFF) * LN2C))
                v.memset(consts[:, 2:3], float(-LN2C))
                v.memset(consts[:, 3:4], float(-(3 + OFF) * LN2C))
                v.memset(consts[:, 4:5], float(LN2C))
                v.memset(consts[:, 5:6], float((10 + OFF) * LN2C))
                v.memset(consts[:, 6:7], float(-(10 + OFF) * LN2C))
                v.memset(ones_bf[:], 1.0)
                sV.inc(v.memset(ones_f[:], 1.0))

            consts_ready = sV.n

            @block.gpsimd
            def _(g):
                for dt in range(ND):
                    sWA.incd(g.dma_start(wA[:, dt * D:(dt + 1) * D],
                                         w_d["wk"][dt * 128:(dt + 1) * 128, :]))
                for dt in range(ND):
                    sWB.incd(g.dma_start(wB[:, dt * D:(dt + 1) * D],
                                         w_d["wv"][dt * 128:(dt + 1) * 128, :]))
                for k in ["bq", "bk", "bv", "bu", "bo"]:
                    sROW.incd(g.dma_start(brow[k][:], b_d[k][None, :]))
                sROW.incd(g.dma_start(b1row[:], b1_d[None, :]))
                sROW.incd(g.dma_start(b2f[:], b_d["b2"][None, :]))
                sROW.incd(g.dma_start(lnw_r[:], lnw_d[None, :]))
                sROW.incd(g.dma_start(lnb_r[:], lnb_d[None, :]))
                for i in range(4):
                    sW1Q.incd(g.dma_start(
                        s_w1q[i * FF // 4:(i + 1) * FF // 4, :],
                        w1_d[i * FF // 4:(i + 1) * FF // 4, :]))

            gd_wk = sWA.n
            gd_wv = sWB.n
            gd_rows = sROW.n
            gd_w1 = sW1Q.n

            @block.tensor
            def _(te):
                sV.wait(te, consts_ready)
                sROW.wait(te, gd_rows)
                for h in range(2):
                    te.matmul(ps_big[0][:, h * 512:(h + 1) * 512],
                              ones_f[0:1, 0:128], lnw_r[0:1, h * 512:(h + 1) * 512],
                              start=True, stop=True)
                    te.matmul(ps_big[1][:, h * 512:(h + 1) * 512],
                              ones_f[0:1, 0:128], lnb_r[0:1, h * 512:(h + 1) * 512],
                              start=True, stop=True)
                te.matmul(ps_sm[0][:, 0:512], ones_f[0:1, 0:128],
                          b2f[0:1, 0:512], start=True, stop=True)
                sT.inc(te.matmul(ps_sm[1][:, 0:512], ones_f[0:1, 0:128],
                                 b2f[0:1, 512:1024], start=True, stop=True))

            pre_mm = sT.n

            @block.vector
            def _(v):
                sT.wait(v, pre_mm)
                v.tensor_copy(lnw_b[:], ps_big[0][:, 0:1024])
                v.tensor_copy(lnb_b[:], ps_big[1][:, 0:1024])
                v.tensor_copy(aoW[:, 0:512], ps_sm[0][:, 0:512])
                sEv.inc(v.tensor_copy(aoW[:, 512:1024], ps_sm[1][:, 0:512]))

            pre_bc = sEv.n

            @block.scalar
            def _(sc):
                sEv.wait(sc, pre_bc)
                sS.inc(sc.sign(aoW[:, 1024:2048], aoW[:, 0:1024]))

            b2sgc = sS.n

            @block.vector
            def _(v):
                sS.wait(v, b2sgc)
                sV.inc(v.tensor_tensor(aoW[:, 2048:3072], aoW[:, 0:1024],
                                       aoW[:, 1024:2048], ALU.mult))

            b2abc = sV.n
            for h in range(2):
                ch_w2.emit(aoW[:, 2048 + h * 512: 2048 + (h + 1) * 512],
                           w2q_o[:, h * 512:(h + 1) * 512], 10,
                           a_waits=[(sV, b2abc)],
                           sg_ap=aoW[:, 1024 + h * 512: 1024 + (h + 1) * 512])

            b2chain_v = sV.n

            @block.scalar
            def _(sc):
                sV.wait(sc, b2chain_v)
                sS.inc(sc.copy(b2qrow[0:1, :], w2q_o[0:1, 0:1024]))

            b2q_done_s = sS.n

            w2_tiles = [(pt, cc) for pt in range(8) for cc in range(8)]
            w2_state = {"idx": 0, "slot_free_v": [b2chain_v, b2chain_v],
                        "store_cnt": [0, 0], "sg_free_v": b2chain_v}

            def emit_w2_quant(ntiles):
                for _ in range(ntiles):
                    idx = w2_state["idx"]
                    if idx >= 64:
                        return
                    w2_state["idx"] += 1
                    pt, cc = w2_tiles[idx]
                    slot = idx % 2
                    sfree = w2_state["slot_free_v"][slot]
                    sgfree = w2_state["sg_free_v"]

                    @block.gpsimd
                    def _(g, pt=pt, cc=cc, slot=slot, sfree=sfree):
                        sV.wait(g, sfree)
                        sW2L[slot].incd(g.dma_start(
                            w2in[:, slot * 512:(slot + 1) * 512],
                            w2_d[pt * 128:(pt + 1) * 128,
                                 cc * 512:(cc + 1) * 512]))

                    ldc = sW2L[slot].n

                    @block.scalar
                    def _(sc, slot=slot, ldc=ldc, sgfree=sgfree):
                        sW2L[slot].wait(sc, ldc)
                        sV.wait(sc, sgfree)
                        sS.inc(sc.sign(w2sg[:],
                                       w2in[:, slot * 512:(slot + 1) * 512]))

                    sgc = sS.n

                    @block.vector
                    def _(v, slot=slot, sgc=sgc):
                        sS.wait(v, sgc)
                        sV.inc(v.tensor_tensor(
                            w2a[:], w2in[:, slot * 512:(slot + 1) * 512],
                            w2sg[:], ALU.mult))

                    ac = sV.n
                    w2_state["slot_free_v"][slot] = ac
                    ch_w2.emit(w2a[:], w2q_o[:, slot * 512:(slot + 1) * 512], 3,
                               a_waits=[(sV, ac)], sg_ap=w2sg[:],
                               out_waits=[(sW2S[slot],
                                           w2_state["store_cnt"][slot]),
                                          (sS, b2q_done_s)])
                    w2_state["sg_free_v"] = sV.n
                    done = sV.n

                    @block.sync
                    def _(sy, pt=pt, cc=cc, slot=slot, done=done):
                        sV.wait(sy, done)
                        sW2S[slot].incd(sy.dma_start(
                            s_w2q[pt * 128:(pt + 1) * 128,
                                  cc * 512:(cc + 1) * 512],
                            w2q_o[:, slot * 512:(slot + 1) * 512]))

                    w2_state["store_cnt"][slot] = sW2S[slot].n

            x1_store = [0] * NCH
            for c in range(NCH):
                @block.gpsimd
                def _(g, c=c, vnow=sV.n, snow=sS.n):
                    sV.wait(g, vnow)
                    sS.wait(g, snow)
                    for i in range(4):
                        nt = 4 * c + i
                        sX.incd(g.dma_start(
                            x_ch[:, i * D:(i + 1) * D],
                            x_d[nt * 128:(nt + 1) * 128, :]))

                gd_xc = sX.n

                def ap_x(i):
                    return x_ch[:, i * D:(i + 1) * D]

                def ap_x1(i):
                    return xbuf[:, i * D:(i + 1) * D]

                def apply1(v, i, r_col, mr_col):
                    last = v.tensor_scalar(ap_x1(i), ap_x(i), r_col, mr_col,
                                           ALU.mult, ALU.subtract)
                    if i == 3:
                        sV.inc(last)

                emit_ln(ap_x, apply1, x_waits=[(sX, gd_xc)],
                        apply_waits=[(sST, x1_store[c - 1] if c else 0)])
                x1c = sV.n

                @block.sync
                def _(sy, c=c, x1c=x1c):
                    sV.wait(sy, x1c)
                    for i in range(4):
                        nt = 4 * c + i
                        sST.incd(sy.dma_start(
                            s_x1[nt * 128:(nt + 1) * 128, :], ap_x1(i)))

                x1_store[c] = sST.n

                @block.sync
                def _(sy, c=c):
                    sST.wait(sy, x1_store[c])
                    sT.wait(sy, sT.n)
                    for dt in range(ND):
                        sTL.incd(sy.dma_start_transpose(
                            x1T[:, dt * CH:(dt + 1) * CH],
                            s_x1[c * CH:(c + 1) * CH, dt * 128:(dt + 1) * 128]))

                x1Tc = sTL.n

                def mk_kv(weight_sb, bias_name):
                    def mm(te, k, ps, weight_sb=weight_sb, bias_name=bias_name):
                        nsl, jc = divmod(k, 2)
                        for dt in range(ND):
                            te.matmul(
                                ps[:, 0:512],
                                x1T[:, dt * CH + nsl * 128:
                                    dt * CH + nsl * 128 + 128],
                                weight_sb[:, dt * D + jc * 512:
                                          dt * D + jc * 512 + 512],
                                start=(dt == 0), stop=False)
                        return te.matmul(
                            ps[:, 0:512], ones_bf[0:1, 0:128],
                            brow[bias_name][0:1, jc * 512:jc * 512 + 512],
                            start=False, stop=True)
                    return mm

                def ev_k(e, k, ps):
                    nsl, jc = divmod(k, 2)
                    return e.activation(
                        k_sb[:, nsl * D + jc * 512: nsl * D + jc * 512 + 512],
                        ps[:, 0:512], AF.Relu)

                pipelined_gemm(8, ps_sm, mk_kv(wA, "bk"), "scalar", ev_k, sEs,
                               pre_waits=[(sTL, x1Tc), (sWA, gd_wk),
                                          (sV, consts_ready)])

                def ev_v(e, k, ps):
                    nsl, jc = divmod(k, 2)
                    return e.tensor_copy(
                        v_sb[:, nsl * D + jc * 512: nsl * D + jc * 512 + 512],
                        ps[:, 0:512])

                pipelined_gemm(8, ps_sm, mk_kv(wB, "bv"), "vector", ev_v, sEv,
                               pre_waits=[(sWB, gd_wv)])

                kvev_s, kvev_v = sEs.n, sEv.n

                @block.tensor
                def _(te, c=c, kvev_s=kvev_s, kvev_v=kvev_v):
                    sEs.wait(te, kvev_s)
                    sEv.wait(te, kvev_v)
                    last = None
                    for p in range(8):
                        for nsl in range(4):
                            last = te.matmul(
                                ps_kv[:, p * 128:(p + 1) * 128],
                                k_sb[:, nsl * D + p * 128:
                                     nsl * D + p * 128 + 128],
                                v_sb[:, nsl * D + p * 128:
                                     nsl * D + p * 128 + 128],
                                start=(c == 0 and nsl == 0 and p % 4 == 0),
                                stop=(c == NCH - 1 and nsl == 3 and p % 4 == 3))
                    sT.inc(last)

                emit_w2_quant(10)

            kv_done = sT.n

            @block.vector
            def _(v):
                sV.inc(v.memset(kvblk[:], 0.0))

            for p in range(8):
                @block.scalar
                def _(sc, p=p, vnow=sV.n):
                    sT.wait(sc, kv_done)
                    sV.wait(sc, vnow)
                    sS.inc(sc.sign(kvsg[:], ps_kv[:, p * 128:(p + 1) * 128]))

                sgc = sS.n

                @block.vector
                def _(v, p=p, sgc=sgc):
                    sS.wait(v, sgc)
                    v.tensor_tensor(kvt1[:], ps_kv[:, p * 128:(p + 1) * 128],
                                    kvsg[:], ALU.mult)
                    v.tensor_scalar(kvt2[:], kvt1[:], 0.01, 100.0,
                                    ALU.max, ALU.min)
                    v.tensor_tensor(kvt1[:], kvt2[:], kvsg[:], ALU.mult)
                    v.tensor_copy(kvblk[0:64, p * 128:p * 128 + 64],
                                  kvt1[0:64, 0:64])
                    sV.inc(v.tensor_copy(
                        kvblk[64:128, p * 128 + 64:p * 128 + 128],
                        kvt1[64:128, 64:128]))

            kvblk_done = sV.n

            @block.gpsimd
            def _(g):
                sT.wait(g, kv_done)
                for dt in range(ND):
                    sWA.incd(g.dma_start(wA[:, dt * D:(dt + 1) * D],
                                         w_d["wq"][dt * 128:(dt + 1) * 128, :]))
                for dt in range(ND):
                    sWB.incd(g.dma_start(wB[:, dt * D:(dt + 1) * D],
                                         w_d["wu"][dt * 128:(dt + 1) * 128, :]))

            gd_wq = sWA.n
            gd_wu = sWB.n

            uln_store = [0] * NCH
            for c in range(NCH):
                @block.sync
                def _(sy, c=c):
                    sT.wait(sy, sT.n)
                    for dt in range(ND):
                        sTL.incd(sy.dma_start_transpose(
                            x1T[:, dt * CH:(dt + 1) * CH],
                            s_x1[c * CH:(c + 1) * CH, dt * 128:(dt + 1) * 128]))

                x1Tc = sTL.n

                def mm_q(te, k, ps):
                    jt = k
                    for dt in range(ND):
                        te.matmul(
                            ps[:, 0:512],
                            wA[:, dt * D + jt * 128: dt * D + jt * 128 + 128],
                            x1T[:, dt * CH: dt * CH + 512],
                            start=(dt == 0), stop=False)
                    return te.matmul(
                        ps[:, 0:512], brow["bq"][0:1, jt * 128:jt * 128 + 128],
                        ones_bf[0:1, 0:512], start=False, stop=True)

                def ev_q(e, k, ps):
                    return e.activation(qT_ch[:, k * 512:(k + 1) * 512],
                                        ps[:, 0:512], AF.Relu)

                pipelined_gemm(8, ps_sm, mm_q, "scalar", ev_q, sEs,
                               pre_waits=[(sTL, x1Tc), (sWA, gd_wq),
                                          (sV, kvblk_done)])

                def mm_ao(te, k, ps):
                    nsl = k
                    last = None
                    for p in range(8):
                        last = te.matmul(
                            ps[:, p * 128:(p + 1) * 128],
                            qT_ch[:, p * 512 + nsl * 128:
                                  p * 512 + nsl * 128 + 128],
                            kvblk[:, p * 128:(p + 1) * 128],
                            start=(p % 4 == 0), stop=(p % 4 == 3))
                    return last

                def ev_ao(e, k, ps):
                    return e.tensor_copy(aoW[:, k * D:(k + 1) * D],
                                         ps[:, 0:1024])

                pipelined_gemm(4, ps_big, mm_ao, "vector", ev_ao, sEv)
                ao_done = sEv.n

                def ap_ao(i):
                    return aoW[:, i * D:(i + 1) * D]

                def apply_attn(v, i, r_col, mr_col):
                    v.tensor_scalar(t1buf[:], ap_ao(i), r_col, mr_col,
                                    ALU.mult, ALU.subtract)
                    v.tensor_tensor(t2buf[:], t1buf[:], lnw_b[:], ALU.mult)
                    last = v.tensor_tensor(ap_ao(i), t2buf[:], lnb_b[:],
                                           ALU.add)
                    if i == 3:
                        sV.inc(last)

                emit_ln(ap_ao, apply_attn, x_waits=[(sEv, ao_done)])
                t3_done = sV.n

                def mm_u(te, k, ps):
                    nsl, jc = divmod(k, 2)
                    for dt in range(ND):
                        te.matmul(
                            ps[:, 0:512],
                            x1T[:, dt * CH + nsl * 128:
                                dt * CH + nsl * 128 + 128],
                            wB[:, dt * D + jc * 512: dt * D + jc * 512 + 512],
                            start=(dt == 0), stop=False)
                    return te.matmul(
                        ps[:, 0:512], ones_bf[0:1, 0:128],
                        brow["bu"][0:1, jc * 512:jc * 512 + 512],
                        start=False, stop=True)

                def ev_u(e, k, ps):
                    nsl, jc = divmod(k, 2)
                    return e.tensor_tensor(
                        uln_ch[:, nsl * D + jc * 512: nsl * D + jc * 512 + 512],
                        ps[:, 0:512],
                        aoW[:, nsl * D + jc * 512: nsl * D + jc * 512 + 512],
                        ALU.mult)

                pipelined_gemm(8, ps_sm, mm_u, "vector", ev_u, sEv,
                               pre_waits=[(sWB, gd_wu), (sV, t3_done),
                                          (sST, uln_store[c - 1] if c else 0)])
                uln_done = sEv.n

                @block.sync
                def _(sy, c=c, uln_done=uln_done):
                    sEv.wait(sy, uln_done)
                    for i in range(4):
                        nt = 4 * c + i
                        sST.incd(sy.dma_start(
                            s_uln[nt * 128:(nt + 1) * 128, :],
                            uln_ch[:, i * D:(i + 1) * D]))

                uln_store[c] = sST.n
                emit_w2_quant(6)

            @block.gpsimd
            def _(g):
                sT.wait(g, sT.n)
                for dt in range(ND):
                    sWA.incd(g.dma_start(wA[:, dt * D:(dt + 1) * D],
                                         w_d["wo"][dt * 128:(dt + 1) * 128, :]))

            gd_wo = sWA.n

            xm_store = [0] * NCH
            x2_store = [0] * NCH
            ln2_apply = [0] * NCH
            ln2_sq = [0] * NCH
            for c in range(NCH):
                @block.gpsimd
                def _(g, c=c, vnow=sV.n, snow=sS.n, evnow=sEv.n):
                    sV.wait(g, vnow)
                    sS.wait(g, snow)
                    sEv.wait(g, evnow)
                    for i in range(4):
                        nt = 4 * c + i
                        sX.incd(g.dma_start(
                            x_ch[:, i * D:(i + 1) * D],
                            x_d[nt * 128:(nt + 1) * 128, :]))

                gd_xc = sX.n

                @block.sync
                def _(sy, c=c):
                    sST.wait(sy, uln_store[c])
                    sT.wait(sy, sT.n)
                    for dt in range(ND):
                        sTL.incd(sy.dma_start_transpose(
                            ulnT_ch[:, dt * CH:(dt + 1) * CH],
                            s_uln[c * CH:(c + 1) * CH,
                                  dt * 128:(dt + 1) * 128]))

                ulnTc = sTL.n

                def mm_z(te, k, ps):
                    nsl, jc = divmod(k, 2)
                    for dt in range(ND):
                        te.matmul(
                            ps[:, 0:512],
                            ulnT_ch[:, dt * CH + nsl * 128:
                                    dt * CH + nsl * 128 + 128],
                            wA[:, dt * D + jc * 512: dt * D + jc * 512 + 512],
                            start=(dt == 0), stop=False)
                    return te.matmul(
                        ps[:, 0:512], ones_bf[0:1, 0:128],
                        brow["bo"][0:1, jc * 512:jc * 512 + 512],
                        start=False, stop=True)

                def ev_z(e, k, ps):
                    nsl, jc = divmod(k, 2)
                    off = nsl * D + jc * 512
                    return e.tensor_tensor(aoW[:, off:off + 512], ps[:, 0:512],
                                           x_ch[:, off:off + 512], ALU.add)

                prevw = [(sX, gd_xc)]
                if c > 0:
                    prevw += [(sST, max(xm_store[c - 1], x2_store[c - 1])),
                              (sV, ln2_apply[c - 1]), (sS, ln2_sq[c - 1])]
                pipelined_gemm(8, ps_sm, mm_z, "vector", ev_z, sEv,
                               pre_waits=[(sTL, ulnTc), (sWA, gd_wo)] + prevw)
                xmid_done = sEv.n

                @block.sync
                def _(sy, c=c, xmid_done=xmid_done):
                    sEv.wait(sy, xmid_done)
                    for i in range(4):
                        nt = 4 * c + i
                        sST.incd(sy.dma_start(
                            s_xmid[nt * 128:(nt + 1) * 128, :],
                            aoW[:, i * D:(i + 1) * D]))

                xm_store[c] = sST.n

                def ap_xm(i):
                    return aoW[:, i * D:(i + 1) * D]

                def ap_x2o(i):
                    return xbuf[:, i * D:(i + 1) * D]

                def apply2(v, i, r_col, mr_col):
                    last = v.tensor_scalar(ap_x2o(i), ap_xm(i), r_col, mr_col,
                                           ALU.mult, ALU.subtract)
                    if i == 3:
                        sV.inc(last)

                emit_ln(ap_xm, apply2,
                        x_waits=[(sEv, xmid_done)],
                        apply_waits=[(sST, x2_store[c - 1] if c
                                      else x1_store[NCH - 1])])
                ln2_apply[c] = sV.n
                ln2_sq[c] = sS.n
                x2c = sV.n

                @block.sync
                def _(sy, c=c, x2c=x2c):
                    sV.wait(sy, x2c)
                    for i in range(4):
                        nt = 4 * c + i
                        sST.incd(sy.dma_start(
                            s_x2[nt * 128:(nt + 1) * 128, :],
                            xbuf[:, i * D:(i + 1) * D]))

                x2_store[c] = sST.n

            emit_w2_quant(64)
            r1_final = dict(T=sT.n, S=sS.n, V=sV.n, G=sG.n, Es=sEs.n,
                            Ev=sEv.n, ST=sST.n, TL=sTL.n, GD=sGD.n)

        r2s = ExitStack()
        with r2s:
            w1T = sb_(r2s, "w1T", [128, ND * FF], BF16)
            w2qT = sb_(r2s, "w2qT", [128, NHT * D], BF16)
            x2T_ch = sb_(r2s, "x2T_ch", [128, ND * CF], BF16)
            hq = sb_(r2s, "hq", [128, NHT * CF], BF16)
            abuf = sb_(r2s, "abuf", [128, 2 * 512], BF16)
            ch_f = Chain(r2s, "chf", 512)
            osg = sb_(r2s, "osg", [128, 512], F32)
            oa = sb_(r2s, "oa", [128, 512], F32)
            xm_ch = sb_(r2s, "xm_ch", [128, 2 * D], F32)
            outb = sb_(r2s, "outb", [128, 2 * D], F32)
            ps_h = [r2s.enter_context(
                nc.psum_tensor(f"ps_h{i}", [128, 512], F32)) for i in range(2)]
            ps_o = [r2s.enter_context(
                nc.psum_tensor(f"ps_o{i}", [128, 1024], F32)) for i in range(2)]

            def fence(e):
                sT.wait(e, r1_final["T"])
                sS.wait(e, r1_final["S"])
                sV.wait(e, r1_final["V"])
                sG.wait(e, r1_final["G"])
                sEs.wait(e, r1_final["Es"])
                sEv.wait(e, r1_final["Ev"])

            @block.tensor
            def _(te):
                fence(te)

            @block.scalar
            def _(sc):
                fence(sc)

            @block.vector
            def _(v):
                fence(v)

            @block.gpsimd
            def _(g):
                fence(g)
                sST.wait(g, r1_final["ST"])

            @block.sync
            def _(sy):
                fence(sy)
                sST.wait(sy, r1_final["ST"])
                sW1Q.wait(sy, gd_w1)
                for dt in range(ND):
                    sTL.incd(sy.dma_start_transpose(
                        w1T[:, dt * FF:(dt + 1) * FF],
                        s_w1q[:, dt * 128:(dt + 1) * 128]))
                for ht in range(NHT):
                    sTL.incd(sy.dma_start_transpose(
                        w2qT[:, ht * D:(ht + 1) * D],
                        s_w2q[:, ht * 128:(ht + 1) * 128]))

            w12T_done = sTL.n

            out_store = 0
            for c in range(NCF):
                @block.sync
                def _(sy, c=c):
                    sT.wait(sy, sT.n)
                    for dt in range(ND):
                        sTL.incd(sy.dma_start_transpose(
                            x2T_ch[:, dt * CF:(dt + 1) * CF],
                            s_x2[c * CF:(c + 1) * CF,
                                 dt * 128:(dt + 1) * 128]))

                x2Tc = sTL.n

                @block.gpsimd
                def _(g, c=c, vnow=sV.n):
                    sV.wait(g, vnow)
                    for i in range(2):
                        nt = 2 * c + i
                        sX.incd(g.dma_start(
                            xm_ch[:, i * D:(i + 1) * D],
                            s_xmid[nt * 128:(nt + 1) * 128, :]))

                xm_c = sX.n

                tbase = sT.n
                ebase = sEs.n
                es0, ev0 = sEs.n, sEv.n

                @block.tensor
                def _(te, c=c, tbase=tbase, ebase=ebase, x2Tc=x2Tc,
                      es0=es0, ev0=ev0):
                    sTL.wait(te, x2Tc)
                    if c == 0:
                        sTL.wait(te, w12T_done)
                    sEs.wait(te, es0)
                    sEv.wait(te, ev0)
                    for k in range(NHT):
                        if k >= 2:
                            sEs.wait(te, ebase + k - 1)
                        for dt in range(ND):
                            te.matmul(
                                ps_h[k % 2][:, 0:CF],
                                w1T[:, dt * FF + k * 128:
                                    dt * FF + k * 128 + 128],
                                x2T_ch[:, dt * CF:(dt + 1) * CF],
                                start=(dt == 0), stop=False)
                        sT.inc(te.matmul(
                            ps_h[k % 2][:, 0:CF],
                            b1row[0:1, k * 128:k * 128 + 128],
                            ones_bf[0:1, 0:CF], start=False, stop=True))

                for b in range(16):
                    abuf_free = (ch_f.v2done[-2] if len(ch_f.v2done) >= 2
                                 else 0)

                    @block.scalar
                    def _(sc, b=b, tbase=tbase, abuf_free=abuf_free):
                        sV.wait(sc, abuf_free)
                        for j in range(2):
                            k = b * 2 + j
                            sT.wait(sc, tbase + k + 1)
                            sEs.inc(sc.activation(
                                abuf[:, (b % 2) * 512 + j * CF:
                                     (b % 2) * 512 + (j + 1) * CF],
                                ps_h[k % 2][:, 0:CF], AF.Relu))

                    ch_f.emit(abuf[:, (b % 2) * 512:(b % 2) * 512 + 512],
                              hq[:, b * 512:(b + 1) * 512], 3)

                hq_done = sV.n
                t2base = sT.n
                sgS = sS.n

                @block.tensor
                def _(te, c=c, hq_done=hq_done, sgS=sgS):
                    sV.wait(te, hq_done)
                    sS.wait(te, max(sgS, b2q_done_s))
                    for k in range(4):
                        ns, jc = divmod(k, 2)
                        ps = ps_o[ns]
                        for ht in range(NHT):
                            te.matmul(
                                ps[:, jc * 512:(jc + 1) * 512],
                                hq[:, ht * CF + ns * 128:
                                   ht * CF + ns * 128 + 128],
                                w2qT[:, ht * D + jc * 512:
                                     ht * D + jc * 512 + 512],
                                start=(ht == 0), stop=False)
                        sT.inc(te.matmul(
                            ps[:, jc * 512:(jc + 1) * 512],
                            ones_bf[0:1, 0:128],
                            b2qrow[0:1, jc * 512:jc * 512 + 512],
                            start=False, stop=True))

                for k in range(4):
                    ns, jc = divmod(k, 2)
                    psl = ps_o[ns][:, jc * 512:(jc + 1) * 512]
                    osg_free = ch_f.v2done[-1] if ch_f.v2done else 0

                    @block.scalar
                    def _(sc, k=k, psl=psl, t2base=t2base, osg_free=osg_free):
                        sT.wait(sc, t2base + k + 1)
                        sV.wait(sc, osg_free)
                        sS.inc(sc.sign(osg[:], psl))

                    sgc = sS.n

                    @block.vector
                    def _(v, k=k, psl=psl, sgc=sgc, xm_c=xm_c,
                          out_store=out_store):
                        sS.wait(v, sgc)
                        sX.wait(v, xm_c)
                        sOUT.wait(v, out_store)
                        sV.inc(v.tensor_tensor(oa[:], psl, osg[:], ALU.mult))

                    oac = sV.n
                    ch_f.emit(oa[:],
                              outb[:, ns * D + jc * 512:
                                   ns * D + jc * 512 + 512],
                              10, a_waits=[(sV, oac)], sg_ap=osg[:],
                              add_ap=xm_ch[:, ns * D + jc * 512:
                                           ns * D + jc * 512 + 512])

                och_done = sV.n

                @block.sync
                def _(sy, c=c, och_done=och_done):
                    sV.wait(sy, och_done)
                    for i in range(2):
                        nt = 2 * c + i
                        sOUT.incd(sy.dma_start(
                            out_d[nt * 128:(nt + 1) * 128, :],
                            outb[:, i * D:(i + 1) * D]))

                out_store = sOUT.n

            @block.sync
            def _(sy):
                sOUT.wait(sy)

        for s in sems:
            assert s.maxw <= s.n, f"sem {s.name}: wait {s.maxw} > total {s.n}"

    return nc


_NC_CACHE = []


def kernel(**inputs):
    x = np.ascontiguousarray(np.asarray(inputs["L0_attn_query"],
                                        dtype=np.float32))
    name_map = {
        "wq": "L0_attn_Wq", "bq": "L0_attn_bq", "wk": "L0_attn_Wk",
        "bk": "L0_attn_bk", "wv": "L0_attn_Wv", "bv": "L0_attn_bv",
        "wu": "L0_attn_Wu", "bu": "L0_attn_bu", "wo": "L0_attn_Wo",
        "bo": "L0_attn_bo", "lnw": "L0_attn_ln_w", "lnb": "L0_attn_ln_b",
        "w1": "L0_ffn_w1", "b1": "L0_ffn_b1", "w2": "L0_ffn_w2",
        "b2": "L0_ffn_b2",
    }
    shared = {k: np.ascontiguousarray(np.asarray(inputs[vn], dtype=np.float32))
              for k, vn in name_map.items()}
    if not _NC_CACHE:
        _NC_CACHE.append(build_program())
    nc = _NC_CACHE[0]
    in_maps = []
    for i in range(8):
        m = {"x": np.ascontiguousarray(x[i])}
        m.update(shared)
        in_maps.append(m)
    res = run_bass_kernel_spmd(nc, in_maps, core_ids=list(range(8)))
    return np.stack([np.asarray(res.results[i]["out"]) for i in range(8)],
                    axis=0).astype(np.float32)


if __name__ == "__main__":
    print("building...")
    build_program()
    print("build ok")
